# version 01 (final)
# baseline (speedup 1.0000x reference)
import numpy as np
import ml_dtypes
import concourse.bass as bass
import concourse.mybir as mybir
from concourse.bass_utils import run_bass_kernel_spmd

BF = ml_dtypes.bfloat16
F32 = mybir.dt.float32
BF16 = mybir.dt.bfloat16
ALU = mybir.AluOpType
AF = mybir.ActivationFunctionType
AX = mybir.AxisListType.X
ds = bass.ds

D = 1024
KC = 8
T_SEQ = 2048
HALF = 1024
NTT = 8
NE = 8
EPS = 1e-5

OFF_XT = 0
OFF_XT2 = 2048
OFF_WV = 0
OFF_XN2F = 8192
OFF_QF = 16384
OFF_WQK = 24576
OFF_ELU = 26624
OFF_ST = 0
OFF_W2AT = 8192
OFF_YN = 24576
OFF_KST = 28672
OFF_QST = 32768
OFF_VST = 33792
OFF_W1E = 0
OFF_W2E = 24576
OFF_SL = 30720
OFF_A = 36864
OFF_WS2 = 0
OFF_GS = 12288
OFF_AS = 24576
OFF_WS1 = 36864
SHARED_N = 43008

C_SUM, C_SQ, C_MU, C_MSQ, C_VAR, C_VEPS, C_RV, C_R1 = 0, 8, 16, 24, 32, 40, 48, 56
C_VAR2, C_V2E, C_RV2, C_R2, C_A2, C_B1, C_B2, C_MU2 = 64, 72, 80, 88, 96, 104, 112, 120
E_ESC, E_NMX, E_ESUM, E_M1, E_RS, E_EQ, E_X2, E_M2 = 0, 64, 72, 80, 88, 96, 160, 224
ESCA_N = 232

PE_LN = 16
PQ = 16
PA = 64
PG = 656
PX = 664
PW = 984
D_PE = 1048
VQ = 20
VA = 100
VG = 468
VX = 477
VW = 701
D_EV = 741
AQ = 35
AA = 99
AG = 387
AX_ = 395
AW = 491
D_EA = 515
D_EG = 32
D_SX = 32


def make_sw_ledger(dbg, PH=6):
    L = {}
    c = 0

    def adv(name):
        nonlocal c
        c += 16
        L[name] = c
    if dbg:
        adv("dstats"); adv("dxt")
    if PH >= 2:
        adv("wv")
        for oc in range(16):
            adv(f"wqk{oc}")
        if dbg:
            adv("dxnf"); adv("dxn2f")
        adv("w2at")
    if dbg and PH >= 3:
        adv("dqf"); adv("dkf"); adv("dvag"); adv("dst"); adv("dyn"); adv("dzatt")
    if dbg and PH >= 4:
        adv("dwtm")
    L["exp0"] = c
    if PH >= 5:
        for e in range(NE):
            adv(f"w1e{e}"); adv(f"w2e{e}")
    if PH >= 6:
        adv("ws2")
        for li in range(24):
            adv(f"ws1_{li}")
    L["D"] = c
    return L


def ledger_totals(PH):
    d_pe = [16, 64, 656, 664, 984, 1048][PH - 1]
    d_ev = [20, 100, 468, 477, 701, 741][PH - 1]
    d_ea = [35, 99, 387, 395, 491, 515][PH - 1]
    d_eg = [0, 0, 32, 32, 32, 32][PH - 1]
    return d_pe, d_ev, d_ea, d_eg


def g3(ap_, g=8):
    return ap_.rearrange("p (g x) -> p g x", g=g)


def bc3(col_ap, n):
    return bass.AP(col_ap.tensor, col_ap.offset, [col_ap.ap[0], col_ap.ap[1], [0, n]])


def ea_xn(tt):
    return 12 if tt == 0 else 11 + 3 * tt


def ea_xn2(tt):
    return ea_xn(tt) + 1


def ea_evx2(tt):
    return 16 + 3 * tt if tt < NTT - 1 else 35


class WW:

    def __init__(self, eng, sem, name):
        self.eng = eng
        self.sem = sem
        self.reg = eng.alloc_register(name)
        self.cur = None

    def reset(self, base, off=0):
        self.eng.reg_mov(self.reg, base + off)
        self.cur = off

    def advance(self, off):
        if off > self.cur:
            self.eng.reg_add(self.reg, self.reg, off - self.cur)
            self.cur = off

    def wait(self, off):
        if off < self.cur:
            return
        if off > self.cur:
            self.eng.reg_add(self.reg, self.reg, off - self.cur)
            self.cur = off
        self.eng.wait_ge(self.sem, self.reg)


def fori_walk(eng, trips, wds, body_fn):
    entries = [(w, w.cur, d) for (w, d) in wds]
    with eng.Fori(0, trips) as it:
        body_fn(it)
        for w, cur0, d in entries:
            w.advance(cur0 + d)
            assert w.cur == cur0 + d, (w.cur, cur0, d)
    for w, cur0, d in entries:
        w.cur = cur0 + trips * d


def build_program(n_halves=4, dbg=False, PH=6):
    nc = bass.Bass("TRN2", target_bir_lowering=False)
    global D_PE, D_EV, D_EA, D_EG, D_DBG, DBG_ZATT
    D_PE, D_EV, D_EA, D_EG = ledger_totals(PH)
    nd = 2 if dbg else 0
    if dbg and PH >= 2:
        nd += 2
    if dbg and PH >= 3:
        nd += 6
    DBG_ZATT = nd
    if dbg and PH >= 4:
        nd += 1
    D_DBG = 16 * nd

    x = nc.dram_tensor("x", [2, T_SEQ, D], F32, kind="ExternalInput").ap()
    w1aT = nc.dram_tensor("w1aT", [D, 3 * D], F32, kind="ExternalInput").ap()
    w2aT = nc.dram_tensor("w2aT", [D, D], F32, kind="ExternalInput").ap()
    gwT = nc.dram_tensor("gwT", [D, NE], F32, kind="ExternalInput").ap()
    w1eT = nc.dram_tensor("w1eT", [NE, D, 1536], F32, kind="ExternalInput").ap()
    w2eT = nc.dram_tensor("w2eT", [NE, 768, D], F32, kind="ExternalInput").ap()
    ws1T = nc.dram_tensor("ws1T", [D, 3072], F32, kind="ExternalInput").ap()
    ws2T = nc.dram_tensor("ws2T", [1536, D], F32, kind="ExternalInput").ap()
    identc = nc.dram_tensor("identc", [128, 128], BF16, kind="ExternalInput").ap()
    cmask2 = nc.dram_tensor("cmask2", [128, 896], BF16, kind="ExternalInput").ap()
    out = nc.dram_tensor("out", [2, T_SEQ, D], F32, kind="ExternalOutput").ap()

    dbgo = {}
    if dbg:
        for name, wid in [("d_stats", 128), ("d_xt", 4096), ("d_xnf", KC * HALF), ("d_xn2f", KC * HALF), ("d_qf", KC * HALF),
                          ("d_kf", 8 * T_SEQ), ("d_vag", 16 * 16 * 65), ("d_st", 16 * 512),
                          ("d_yn", 8 * 512), ("d_zatt", NTT * D), ("d_wtm", NTT * NE)]:
            dbgo[name] = nc.dram_tensor(name, [n_halves, 128, wid], F32,
                                        kind="ExternalOutput").ap()

    xr = x.rearrange("s (tt p) d -> p (s tt) d", p=128)
    outr = out.rearrange("s (tt p) d -> p (s tt) d", p=128)

    from contextlib import ExitStack
    with ExitStack() as _es:
        def _c(cm):
            return _es.enter_context(cm)
        Z = _c(nc.sbuf_tensor("Z", [128, NTT * D], F32))
        XNF = _c(nc.sbuf_tensor("XNF", [128, KC * HALF], BF16))
        KF = _c(nc.sbuf_tensor("KF", [128, 8 * T_SEQ], BF16))
        VAG = _c(nc.sbuf_tensor("VAG", [128, 16 * 16 * 65], BF16))
        SH = _c(nc.sbuf_tensor("SH", [128, SHARED_N], BF16))
        SCR32 = _c(nc.sbuf_tensor("SCR32", [128, 1024], F32))
        STATS = _c(nc.sbuf_tensor("STATS", [128, 128], F32))
        ESCA = _c(nc.sbuf_tensor("ESCA", [128, ESCA_N], F32))
        WTM = _c(nc.sbuf_tensor("WTM", [128, NTT * NE], F32))
        WTC = _c(nc.sbuf_tensor("WTC", [128, 8], F32))
        IDT = _c(nc.sbuf_tensor("IDT", [128, 128], BF16))
        GWTS = _c(nc.sbuf_tensor("GWTS", [128, KC * NE], BF16))
        ONESR = _c(nc.sbuf_tensor("ONESR", [1, 128], BF16))
        RCPR = _c(nc.sbuf_tensor("RCPR", [1, 512], BF16))
        RCPB = _c(nc.sbuf_tensor("RCPB", [128, 512], BF16))
        CMR = _c(nc.sbuf_tensor("CMR", [128, 896], BF16))
        PM = _c(nc.psum_tensor("PM", [128, 1024], F32))
        PST = _c(nc.psum_tensor("PST", [128, 1024], F32))
        PY = _c(nc.psum_tensor("PY", [128, 512], F32))
        PBC = _c(nc.psum_tensor("PBC", [128, 512], F32))
        PTR = _c(nc.psum_tensor("PTR", [128, 1024], BF16))
        h_xin = _c(nc.semaphore("s_xin"))
        h_out = _c(nc.semaphore("s_out"))
        h_wv = _c(nc.semaphore("s_wv"))
        h_wq0 = _c(nc.semaphore("s_wq0"))
        h_wq1 = _c(nc.semaphore("s_wq1"))
        h_w2at = _c(nc.semaphore("s_w2at"))
        h_w1e0 = _c(nc.semaphore("s_w1e0"))
        h_w1e1 = _c(nc.semaphore("s_w1e1"))
        h_w2e = _c(nc.semaphore("s_w2e"))
        h_ws1a = _c(nc.semaphore("s_ws1a"))
        h_ws1b = _c(nc.semaphore("s_ws1b"))
        h_ws2 = _c(nc.semaphore("s_ws2"))
        h_dbg = _c(nc.semaphore("s_dbg"))
        h_pe = _c(nc.semaphore("s_pe"))
        h_ev = _c(nc.semaphore("s_ev"))
        h_ea = _c(nc.semaphore("s_ea"))
        h_eg = _c(nc.semaphore("s_eg"))
        block = _c(nc.Block())

        S0 = dict(sx=16, sw=32, pe=0, ev=0, ea=0, eg=1)
        SHa = SH.ap()
        Za = Z.ap()
        KFa = KF.ap()
        XNFa = XNF.ap()
        VAGa = VAG.ap()

        def sha(off, n):
            return SHa[:, off:off + n]

        def pmt(slot, n=512):
            return PM[:, slot * 512:slot * 512 + n]

        @block.sync
        def _(sy):
            sy.dma_start(out=IDT[:], in_=identc).then_inc(h_xin, 16)
            wV = WW(sy, h_ev, "sy_wv")
            with sy.Fori(0, n_halves) as H:
                sy.wait_ge(h_pe, S0["pe"] + H * D_PE)
                sy.wait_ge(h_ea, S0["ea"] + H * D_EA)
                sy.wait_ge(h_eg, S0["eg"] + H * D_EG)
                sy.wait_ge(h_out, 16 * H)
                wV.reset(S0["ev"] + H * D_EV)
                wV.wait(0)
                sy.dma_start(out=g3(Za, NTT), in_=xr[:, ds(H * 8, 8), :]).then_inc(h_xin, 16)
                wV.wait(D_EV)
                sy.dma_start(out=outr[:, ds(H * 8, 8), :], in_=g3(Za, NTT)).then_inc(h_out, 16)
            sy.wait_ge(h_out, 16 * n_halves)
            if dbg:
                sy.wait_ge(h_dbg, n_halves * D_DBG)

        @block.gpsimd
        def _(g):
            jr4 = g.alloc_register("jr4")
            wP = WW(g, h_pe, "g_wp")
            wV = WW(g, h_ev, "g_wv")
            wA = WW(g, h_ea, "g_wa")
            g.dma_start(out=g3(GWTS.ap(), KC),
                        in_=gwT.rearrange("(kc p) e -> p kc e", p=128)).then_inc(h_wv, 16)
            g.dma_start(out=CMR.ap(), in_=cmask2).then_inc(h_w2at, 16)
            g.memset(ONESR[:], 1.0)
            g.memset(KFa, 0.0)
            g.memset(VAGa, 0.0)
            g.drain()
            vag4 = VAGa.rearrange("p (h i c) -> p h i c", h=16, i=16)
            g.memset(vag4[:, :, :, 64:65], 1.0).then_inc(h_eg, 1)

            with g.Fori(0, n_halves) as H:
                ndbg = [0]
                wP.reset(S0["pe"] + H * D_PE)
                wP.wait(0)
                wV.reset(S0["ev"] + H * D_EV)
                wV.wait(0)
                wA.reset(S0["ea"] + H * D_EA)
                wA.wait(0)

                def dbg_dma(name, src_ap):
                    g.wait_ge(h_dbg, H * D_DBG + 16 * ndbg[0])
                    g.dma_start(out=dbgo[name][ds(H, 1)], in_=src_ap).then_inc(h_dbg, 16)
                    ndbg[0] += 1
                if dbg:
                    wV.wait(12)
                    dbg_dma("d_stats", STATS.ap())
                    wA.wait(35)
                    dbg_dma("d_xt", sha(OFF_XT, 4096))
                if PH >= 2:
                    wP.wait(PE_LN)
                    if dbg:
                        g.wait_ge(h_dbg, H * D_DBG + 16 * 2)
                    g.dma_start(out=g3(sha(OFF_WV, KC * 1024), KC),
                                in_=w1aT[:, 2048:3072].rearrange("(kc p) n -> p kc n", p=128)
                                ).then_inc(h_wv, 16)
                    for oc in range(16):
                        if oc >= 2:
                            wP.wait(PQ + 2 * (oc - 2) + 2)
                        g.dma_start(out=g3(sha(OFF_WQK + (oc % 2) * 1024, 1024), KC),
                                    in_=w1aT[:, oc * 128:(oc + 1) * 128]
                                    .rearrange("(kc p) n -> p kc n", p=128)
                                    ).then_inc(h_wq0 if oc % 2 == 0 else h_wq1, 16)
                    if dbg and PH >= 2:
                        wV.wait(VQ)
                        dbg_dma("d_xnf", XNFa)
                        wA.wait(AQ)
                        dbg_dma("d_xn2f", sha(OFF_XN2F, KC * HALF))
                    wP.wait(PQ + 48)
                    if dbg and PH >= 2:
                        g.wait_ge(h_dbg, H * D_DBG + 16 * ndbg[0])
                    g.wait_ge(h_w2at, 16 * H + 16)
                    g.dma_start(out=g3(sha(OFF_W2AT, KC * 1024), KC),
                                in_=w2aT.rearrange("(kc p) n -> p kc n", p=128)
                                ).then_inc(h_w2at, 16)
                if PH >= 3:
                    g.reg_mov(jr4, H % 2)
                    wV.advance(VA)
                    wA.advance(AA)
                    for jj in range(2):
                        for pp in range(2):
                            wV.advance(VA + jj * 184 + pp * 88)
                            wA.advance(AA + jj * 144 + pp * 72)

                            def gp_f(f, jj=jj, pp=pp):
                                for c in range(2):
                                    for k in range(4):
                                        wV.wait(VA + jj * 184 + pp * 88 + c * 10 + 2 * k + 4)
                                        wA.wait(AA + jj * 144 + pp * 72 + c * 9 + 2 * k + 2)

                                        def mask_tt(jabs):
                                            st4 = bass.AP(SH, OFF_ST + (4 * k + 3) * 512,
                                                          [[SHARED_N, 128], [-512, 4], [1, 512]])
                                            if jabs == k:
                                                st4b = bass.AP(SH, OFF_ST + (4 * k + 3) * 512,
                                                               [[SHARED_N, 128], [-512, 4], [1, 512]])
                                                cm4 = bass.AP(CMR, 0,
                                                              [[896, 128], [128, 4], [1, 512]])
                                                ins = g.tensor_tensor(st4, cm4, st4b, ALU.mult)
                                            else:
                                                assert jabs < k
                                                ins = g.memset(st4, 0.0)
                                            if k == 3:
                                                ins.then_inc(h_eg, 1)
                                        if k >= jj + 2:
                                            with g.If_eq(jr4, 0):
                                                mask_tt(jj)
                                            with g.Else():
                                                mask_tt(jj + 2)
                                        elif k >= jj:
                                            if k == 3:
                                                raise AssertionError
                                            with g.If_eq(jr4, 0):
                                                mask_tt(jj)
                            fori_walk(g, 4, [(wV, 22), (wA, 18)], gp_f)
                    wV.advance(VG)
                    wA.advance(AG)
                    if dbg and PH >= 3:
                        wV.wait(VG)
                        wA.wait(AG)
                        dbg_dma("d_qf", sha(OFF_QF, KC * HALF))
                        dbg_dma("d_kf", KFa)
                        dbg_dma("d_vag", VAGa)
                        dbg_dma("d_st", sha(OFF_ST, 16 * 512))
                        dbg_dma("d_yn", sha(OFF_YN, 8 * 512))
                        dbg_dma("d_zatt", Za)
                    if dbg and PH >= 4:
                        wV.wait(VX)
                        dbg_dma("d_wtm", WTM.ap())
                if PH >= 5:
                    wP.wait(PG + 8)
                    if dbg:
                        g.wait_ge(h_dbg, H * D_DBG + 16 * ndbg[0])
                    g.dma_start(out=g3(sha(OFF_W1E, 12288), KC),
                                in_=w1eT[0].rearrange("(kc p) n -> p kc n", p=128)
                                ).then_inc(h_w1e0, 16)
                    g.dma_start(out=sha(OFF_W2E, 6144).rearrange("p (hc n) -> p hc n", hc=6),
                                in_=w2eT[0].rearrange("(hc p) n -> p hc n", p=128)
                                ).then_inc(h_w2e, 16)
                    for e in range(NE - 1):
                        if e >= 1:
                            wP.wait(PX + (e - 1) * 40 + 24)
                        g.dma_start(out=g3(sha(OFF_W1E + ((e + 1) % 2) * 12288, 12288), KC),
                                    in_=w1eT[e + 1].rearrange("(kc p) n -> p kc n", p=128)
                                    ).then_inc(h_w1e1 if (e + 1) % 2 else h_w1e0, 16)
                        wP.wait(PX + e * 40 + 40)
                        g.dma_start(out=sha(OFF_W2E, 6144).rearrange("p (hc n) -> p hc n", hc=6),
                                    in_=w2eT[e + 1].rearrange("(hc p) n -> p hc n", p=128)
                                    ).then_inc(h_w2e, 16)
                if PH >= 6:
                    wP.wait(PX + 7 * 40 + 24)
                    g.dma_start(out=sha(OFF_WS2, 12288).rearrange("p (hc n) -> p hc n", hc=12),
                                in_=ws2T.rearrange("(hc p) n -> p hc n", p=128)).then_inc(h_ws2, 16)
                    cols = [1536 + oc * 128 for oc in range(12)] + [oc * 128 for oc in range(12)]
                    for li, cb in enumerate(cols):
                        if li < 2:
                            wP.wait(PX + 320)
                        else:
                            wP.wait(PW + 2 * (li - 2) + 2)
                        g.dma_start(out=g3(sha(OFF_WS1 + (li % 2) * 1024, 1024), KC),
                                    in_=ws1T[:, cb:cb + 128].rearrange("(kc p) n -> p kc n", p=128)
                                    ).then_inc(h_ws1a if li % 2 == 0 else h_ws1b, 16)

        @block.tensor
        def _(t):
            wA = WW(t, h_ea, "t_wa")
            wV = WW(t, h_ev, "t_wv")
            wG = WW(t, h_eg, "t_wg")
            wQ0 = WW(t, h_wq0, "t_wq0")
            wQ1 = WW(t, h_wq1, "t_wq1")
            wE0 = WW(t, h_w1e0, "t_we0")
            wE1 = WW(t, h_w1e1, "t_we1")
            wWE = WW(t, h_w2e, "t_wwe")
            wSA = WW(t, h_ws1a, "t_wsa")
            wSB = WW(t, h_ws1b, "t_wsb")
            with t.Fori(0, n_halves) as H:
                wV.reset(S0["ev"] + H * D_EV)
                wV.wait(0)
                wA.reset(S0["ea"] + H * D_EA)
                wA.wait(0)
                wG.reset(S0["eg"] + H * D_EG)
                wG.wait(0)
                wQ0.reset(H * 128)
                wQ1.reset(H * 128)
                wE0.reset(H * 64)
                wE1.reset(H * 64)
                wWE.reset(H * 128)
                wSA.reset(H * 192)
                wSB.reset(H * 192)
                t.wait_ge(h_xin, 16 + 16 * (H + 1))
                for tt in range(NTT):
                    wA.wait(ea_xn(tt))
                    if tt >= 1:
                        wA.wait(ea_evx2(tt - 1))
                    for cc in range(KC):
                        ins = t.transpose(
                            PTR[:, cc * 128:(cc + 1) * 128],
                            SHa[:, OFF_XT + (tt % 2) * 1024 + cc * 128:
                                OFF_XT + (tt % 2) * 1024 + cc * 128 + 128], IDT[:])
                    ins.then_inc(h_pe, 1)
                    wA.wait(ea_xn2(tt))
                    wV.wait(13 + tt)
                    for cc in range(KC):
                        ins = t.transpose(
                            PTR[:, cc * 128:(cc + 1) * 128],
                            SHa[:, OFF_XT2 + (tt % 2) * 1024 + cc * 128:
                                OFF_XT2 + (tt % 2) * 1024 + cc * 128 + 128], IDT[:])
                    ins.then_inc(h_pe, 1)
                if PH >= 2:
                    for qk in range(2):
                        for oc2 in range(4):
                            for c in range(2):
                                oc = oc2 * 2 + c
                                gb = qk * 16 + oc * 2
                                ocg = qk * 8 + oc
                                (wQ0 if ocg % 2 == 0 else wQ1).wait(16 * (ocg // 2 + 1))
                                for tb in range(2):
                                    wA.wait(AQ + gb + tb - 1)
                                    for kc in range(KC):
                                        ins = t.matmul(
                                            pmt(tb),
                                            SHa[:, OFF_WQK + c * 1024 + kc * 128:
                                                OFF_WQK + c * 1024 + kc * 128 + 128],
                                            SHa[:, OFF_XN2F + kc * HALF + tb * 512:
                                                OFF_XN2F + kc * HALF + tb * 512 + 512],
                                            start=(kc == 0), stop=(kc == KC - 1))
                                    ins.then_inc(h_pe, 1)
                    wA.wait(AQ + 32)
                    t.wait_ge(h_wv, 16 * (H + 2))
                    for tt in range(NTT):
                        for vb in range(2):
                            wV.wait(VQ + 64 + 2 * tt + vb - 1)
                            for kc in range(KC):
                                ins = t.matmul(
                                    pmt(vb),
                                    SHa[:, OFF_XN2F + kc * HALF + tt * 128:
                                        OFF_XN2F + kc * HALF + tt * 128 + 128],
                                    SHa[:, OFF_WV + kc * 1024 + vb * 512:
                                        OFF_WV + kc * 1024 + vb * 512 + 512],
                                    start=(kc == 0), stop=(kc == KC - 1))
                            ins.then_inc(h_pe, 1)
                if PH >= 3:
                    wV.advance(VA)
                    wA.advance(AA)

                    def t_jj(jj):
                        for pp in range(2):
                            p0 = pp * 64
                            wV.advance(VA + pp * 88)
                            wA.advance(AA + pp * 72)
                            wG.advance(pp * 8)

                            def t_f(f):
                                for c in range(2):
                                    for si in range(16):
                                        if si == 0:
                                            wV.wait(VA + pp * 88 + c + 1)
                                            wV.wait(VA + pp * 88 + 2 + c * 10)
                                            wA.wait(AA + pp * 72 + c * 9)
                                        elif si >= 2:
                                            if si % 2 == 0:
                                                wV.wait(VA + pp * 88 + 2 + c * 10 + si // 2)
                                            else:
                                                wA.wait(AA + pp * 72 + c * 9 + (si - 1) // 2)
                                        t.matmul(
                                            PST[:, (si % 2) * 512:(si % 2) * 512 + 512],
                                            SHa[p0:p0 + 64, OFF_KST + c * 2048 + si * 128:
                                                OFF_KST + c * 2048 + si * 128 + 128],
                                            SHa[p0:p0 + 64, OFF_QST + c * 512:
                                                OFF_QST + c * 512 + 512],
                                            start=True, stop=True).then_inc(h_pe, 1)
                                    wV.wait(VA + pp * 88 + 2 + c * 10 + 8)
                                    wA.wait(AA + pp * 72 + c * 9 + 8)
                                    wG.wait(pp * 8 + c + 1)
                                    for si in range(16):
                                        ins = t.matmul(
                                            PY[0:65, :],
                                            SHa[:, OFF_VST + c * 1040 + si * 65:
                                                OFF_VST + c * 1040 + si * 65 + 65],
                                            SHa[:, OFF_ST + si * 512:OFF_ST + si * 512 + 512],
                                            start=(si == 0), stop=(si == 15))
                                    ins.then_inc(h_pe, 1)
                                    wV.wait(VA + pp * 88 + 2 + c * 10 + 9)
                                    t.matmul(PBC[0:64, :], ONESR[0:1, 0:64], RCPR[0:1, :],
                                             start=True, stop=True).then_inc(h_pe, 1)
                            fori_walk(t, 4, [(wV, 22), (wA, 18), (wG, 2)], t_f)
                        t.wait_ge(h_w2at, 16 * (H + 2))
                        wV.wait(VA + 176)
                        for tt in range(4):
                            for do in range(2):
                                gidx = tt * 2 + do
                                if gidx >= 2:
                                    wV.wait(VA + 176 + gidx - 1)
                                for kc in range(KC):
                                    ins = t.matmul(
                                        pmt(do),
                                        SHa[:, OFF_YN + kc * 512 + tt * 128:
                                            OFF_YN + kc * 512 + tt * 128 + 128],
                                        SHa[:, OFF_W2AT + kc * 1024 + do * 512:
                                            OFF_W2AT + kc * 1024 + do * 512 + 512],
                                        start=(kc == 0), stop=(kc == KC - 1))
                                ins.then_inc(h_pe, 1)
                    fori_walk(t, 2, [(wV, 184), (wA, 144), (wG, 16)], t_jj)
                if PH >= 4:
                    wV.wait(VG)
                    for tt in range(NTT):
                        wA.wait(AG + tt - 1)
                        for kc in range(KC):
                            ins = t.matmul(PM[:, (tt % 2) * 512:(tt % 2) * 512 + 8],
                                           XNFa[:, kc * HALF + tt * 128:kc * HALF + tt * 128 + 128],
                                           GWTS[:, kc * 8:kc * 8 + 8],
                                           start=(kc == 0), stop=(kc == KC - 1))
                        ins.then_inc(h_pe, 1)
                if PH >= 5:
                    wV.advance(VX)
                    wV.wait(VX)
                    wA.advance(AX_)

                    def t_w(w):
                        for c in range(2):
                            (wE0 if c == 0 else wE1).wait(16)
                            wWE.wait(16 * (c + 1))
                            wV.wait(VX + c * 28)
                            wA.wait(AX_ + c * 12)
                            for gc in range(6):
                                for tb in range(2):
                                    wA.wait(AX_ + c * 12 + 2 * gc + tb - 1)
                                    for kc in range(KC):
                                        ins = t.matmul(
                                            pmt(tb),
                                            SHa[:, OFF_W1E + c * 12288 + kc * 1536 + (6 + gc) * 128:
                                                OFF_W1E + c * 12288 + kc * 1536 + (6 + gc) * 128 + 128],
                                            XNFa[:, kc * HALF + tb * 512:kc * HALF + tb * 512 + 512],
                                            start=(kc == 0), stop=(kc == KC - 1))
                                    ins.then_inc(h_pe, 1)
                            wA.wait(AX_ + c * 12 + 12)
                            for yc in range(6):
                                for tb in range(2):
                                    wV.wait(VX + c * 28 + 2 * yc + tb - 1)
                                    for kc in range(KC):
                                        ins = t.matmul(
                                            pmt(tb),
                                            SHa[:, OFF_W1E + c * 12288 + kc * 1536 + yc * 128:
                                                OFF_W1E + c * 12288 + kc * 1536 + yc * 128 + 128],
                                            XNFa[:, kc * HALF + tb * 512:kc * HALF + tb * 512 + 512],
                                            start=(kc == 0), stop=(kc == KC - 1))
                                    ins.then_inc(h_pe, 1)
                            wV.wait(VX + c * 28 + 12)
                            for tb in range(2):
                                for tt4 in range(4):
                                    for do in range(2):
                                        gidx = tb * 8 + tt4 * 2 + do
                                        if gidx >= 2:
                                            wV.wait(VX + c * 28 + 12 + gidx - 1)
                                        for hc in range(6):
                                            ins = t.matmul(
                                                pmt(do),
                                                SHa[:, OFF_A + tb * 3072 + hc * 512 + tt4 * 128:
                                                    OFF_A + tb * 3072 + hc * 512 + tt4 * 128 + 128],
                                                SHa[:, OFF_W2E + hc * 1024 + do * 512:
                                                    OFF_W2E + hc * 1024 + do * 512 + 512],
                                                start=(hc == 0), stop=(hc == 5))
                                        ins.then_inc(h_pe, 1)
                    fori_walk(t, 4, [(wV, 56), (wA, 24), (wE0, 16), (wE1, 16), (wWE, 32)], t_w)
                if PH >= 6:
                    wV.advance(VW)
                    wV.wait(VW)
                    wA.advance(AW)
                    for grp in range(2):
                        if grp == 1:
                            wA.wait(AW + 24)
                        for u in range(6):
                            for c in range(2):
                                oc = 2 * u + c
                                li = grp * 12 + oc
                                (wSA if li % 2 == 0 else wSB).wait(16 * (li // 2 + 1))
                                for tb in range(2):
                                    if grp == 0:
                                        wA.wait(AW + 2 * oc + tb - 1)
                                    else:
                                        wV.wait(VW + 2 * oc + tb - 1)
                                    for kc in range(KC):
                                        ins = t.matmul(
                                            pmt(tb),
                                            SHa[:, OFF_WS1 + c * 1024 + kc * 128:
                                                OFF_WS1 + c * 1024 + kc * 128 + 128],
                                            XNFa[:, kc * HALF + tb * 512:kc * HALF + tb * 512 + 512],
                                            start=(kc == 0), stop=(kc == KC - 1))
                                    ins.then_inc(h_pe, 1)
                    wV.wait(VW + 24)
                    t.wait_ge(h_ws2, 16 * H + 16)
                    for tb in range(2):
                        for tt4 in range(4):
                            for do in range(2):
                                gidx = tb * 8 + tt4 * 2 + do
                                if gidx >= 2:
                                    wV.wait(VW + 24 + gidx - 1)
                                for hc in range(12):
                                    ins = t.matmul(
                                        pmt(do),
                                        SHa[:, OFF_AS + tb * 6144 + hc * 512 + tt4 * 128:
                                            OFF_AS + tb * 6144 + hc * 512 + tt4 * 128 + 128],
                                        SHa[:, OFF_WS2 + hc * 1024 + do * 512:
                                            OFF_WS2 + hc * 1024 + do * 512 + 512],
                                        start=(hc == 0), stop=(hc == 11))
                                ins.then_inc(h_pe, 1)

        @block.scalar
        def _(s):
            wP = WW(s, h_pe, "s_wp")
            wE = WW(s, h_ev, "s_we")
            spr = s.alloc_register("spr")
            with s.Fori(0, n_halves) as H:
                wP.reset(S0["pe"] + H * D_PE)
                wP.wait(0)
                wE.reset(S0["ev"] + H * D_EV)
                wE.wait(0)
                s.wait_ge(h_eg, S0["eg"] + H * D_EG)
                s.wait_ge(h_xin, 16 + 16 * (H + 1))
                if dbg:
                    s.wait_ge(h_dbg, H * D_DBG)
                s.reg_mov(spr, H % 2)
                for tt in range(NTT):
                    if tt >= 1:
                        wE.wait(tt)
                    s.activation(SCR32[:, 0:1024], Z[:, tt * D:(tt + 1) * D],
                                 AF.Square).then_inc(h_ea, 1)
                wE.wait(9)
                s.activation(STATS[:, C_MU2:C_MU2 + 8], STATS[:, C_MU:C_MU + 8],
                             AF.Square).then_inc(h_ea, 1)
                wE.wait(10)
                s.activation(STATS[:, C_R1:C_R1 + 8], STATS[:, C_RV:C_RV + 8],
                             AF.Sqrt).then_inc(h_ea, 1)
                wE.wait(11)
                s.activation(STATS[:, C_R2:C_R2 + 8], STATS[:, C_RV2:C_RV2 + 8],
                             AF.Sqrt).then_inc(h_ea, 1)
                wE.wait(12)
                for tt in range(NTT):
                    if tt >= 2:
                        wP.wait(2 * (tt - 2) + 1)
                    s.activation(SHa[:, OFF_XT + (tt % 2) * 1024:OFF_XT + (tt % 2) * 1024 + 1024],
                                 Z[:, tt * D:(tt + 1) * D], AF.Identity,
                                 bias=STATS[:, C_B1 + tt:C_B1 + tt + 1],
                                 scale=STATS[:, C_R1 + tt:C_R1 + tt + 1]).then_inc(h_ea, 1)
                    if tt >= 2:
                        wP.wait(2 * (tt - 2) + 2)
                    s.activation(SHa[:, OFF_XT2 + (tt % 2) * 1024:OFF_XT2 + (tt % 2) * 1024 + 1024],
                                 Z[:, tt * D:(tt + 1) * D], AF.Identity,
                                 bias=STATS[:, C_B2 + tt:C_B2 + tt + 1],
                                 scale=STATS[:, C_A2 + tt:C_A2 + tt + 1]).then_inc(h_ea, 1)
                    if tt >= 1:
                        wP.wait(2 * (tt - 1) + 2)
                        s.copy(g3(sha(OFF_XN2F, KC * HALF), KC)[:, :, (tt - 1) * 128:tt * 128],
                               g3(PTR.ap(), KC)).then_inc(h_ea, 1)
                wP.wait(2 * (NTT - 1) + 2)
                s.copy(g3(sha(OFF_XN2F, KC * HALF), KC)[:, :, (NTT - 1) * 128:NTT * 128],
                       g3(PTR.ap(), KC)).then_inc(h_ea, 1)
                if PH >= 2:
                    for i in range(32):
                        ocidx, tb = divmod(i, 2)
                        wP.wait(PQ + i + 1)
                        if ocidx < 8:
                            dst = SHa[:, OFF_QF + ocidx * 1024 + tb * 512:
                                      OFF_QF + ocidx * 1024 + tb * 512 + 512]
                            s.copy(dst, pmt(i % 2)).then_inc(h_ea, 1)
                        else:
                            kc0 = (ocidx - 8) * T_SEQ + tb * 512
                            with s.If_eq(spr, 0):
                                s.copy(KFa[:, kc0:kc0 + 512], pmt(i % 2)).then_inc(h_ea, 1)
                            with s.Else():
                                s.copy(KFa[:, kc0 + 1024:kc0 + 1536],
                                       pmt(i % 2)).then_inc(h_ea, 1)
                    for i in range(32):
                        wE.wait(VQ + i + 1 if i < 2 else VQ + 2 * i)
                        s.activation(SHa[:, OFF_ELU + (i % 2) * 1024 + 512:
                                         OFF_ELU + (i % 2) * 1024 + 1024],
                                     SHa[:, OFF_ELU + (i % 2) * 1024:OFF_ELU + (i % 2) * 1024 + 512],
                                     AF.Exp).then_inc(h_ea, 1)
                if PH >= 3:
                    wP.advance(PA)
                    wE.advance(VA)

                    def s_jj(jj):
                        for pp in range(2):
                            wP.advance(PA + pp * 144)
                            wE.advance(VA + pp * 88)

                            def s_f(f):
                                for c in range(2):
                                    for si in range(1, 16, 2):
                                        wP.wait(PA + pp * 144 + c * 18 + si + 1)
                                        s.copy(SHa[:, OFF_ST + si * 512:OFF_ST + si * 512 + 512],
                                               PST[:, 512:1024]).then_inc(h_ea, 1)
                                    wP.wait(PA + pp * 144 + c * 18 + 18)
                                    wE.wait(VA + pp * 88 + 2 + c * 10)
                                    s.copy(RCPB[0:64, :], PBC[0:64, :]).then_inc(h_ea, 1)
                            fori_walk(s, 4, [(wP, 36), (wE, 22)], s_f)
                    fori_walk(s, 2, [(wP, 296), (wE, 184)], s_jj)
                if PH >= 4:
                    for tt in range(NTT):
                        wE.wait(VG + tt + 1)
                        s.activation(ESCA[:, E_ESC + tt * 8:E_ESC + tt * 8 + 8],
                                     PM[:, (tt % 2) * 512:(tt % 2) * 512 + 8],
                                     AF.Exp, bias=ESCA[:, E_NMX + tt:E_NMX + tt + 1]).then_inc(h_ea, 1)
                if PH >= 5:
                    wP.advance(PX)

                    def s_w(w):
                        for c in range(2):
                            for gc in range(6):
                                for tb in range(2):
                                    wP.wait(PX + c * 40 + 2 * gc + tb + 1)
                                    s.activation(SHa[:, OFF_SL + tb * 3072 + gc * 512:
                                                     OFF_SL + tb * 3072 + gc * 512 + 512],
                                                 pmt(tb), AF.Silu).then_inc(h_ea, 1)
                    fori_walk(s, 4, [(wP, 80)], s_w)
                if PH >= 6:
                    wP.advance(PW)
                    for u in range(6):
                        for c in range(2):
                            oc = 2 * u + c
                            for tb in range(2):
                                wP.wait(PW + 2 * oc + tb + 1)
                                s.activation(SHa[:, OFF_GS + tb * 6144 + oc * 512:
                                                 OFF_GS + tb * 6144 + oc * 512 + 512],
                                             pmt(tb), AF.Silu).then_inc(h_ea, 1)

        @block.vector
        def _(v):
            wP = WW(v, h_pe, "v_wp")
            wA2 = WW(v, h_ea, "v_wa")
            vpr = v.alloc_register("vpr")
            with v.Fori(0, n_halves) as H:
                wP.reset(S0["pe"] + H * D_PE)
                wP.wait(0)
                wA2.reset(S0["ea"] + H * D_EA)
                wA2.wait(0)
                v.wait_ge(h_eg, S0["eg"] + H * D_EG)
                v.wait_ge(h_xin, 16 + 16 * (H + 1))
                if dbg:
                    v.wait_ge(h_dbg, H * D_DBG)
                v.reg_mov(vpr, H % 2)
                for tt in range(NTT):
                    v.reduce_sum(STATS[:, C_SUM + tt:C_SUM + tt + 1],
                                 Z[:, tt * D:(tt + 1) * D], axis=AX)
                    wA2.wait(tt + 1)
                    v.reduce_sum(STATS[:, C_SQ + tt:C_SQ + tt + 1],
                                 SCR32[:, 0:1024], axis=AX).then_inc(h_ev, 1)
                v.drain()
                v.tensor_scalar(STATS[:, C_MU:C_MU + 8], STATS[:, C_SUM:C_SUM + 8],
                                1.0 / D, None, ALU.mult)
                v.tensor_scalar(STATS[:, C_MSQ:C_MSQ + 8], STATS[:, C_SQ:C_SQ + 8],
                                1.0 / D, None, ALU.mult).then_inc(h_ev, 1)
                wA2.wait(9)
                v.tensor_sub(STATS[:, C_VAR:C_VAR + 8], STATS[:, C_MSQ:C_MSQ + 8],
                             STATS[:, C_MU2:C_MU2 + 8])
                v.drain()
                v.tensor_scalar(STATS[:, C_VEPS:C_VEPS + 8], STATS[:, C_VAR:C_VAR + 8],
                                EPS, None, ALU.add)
                v.drain()
                v.reciprocal(STATS[:, C_RV:C_RV + 8],
                             STATS[:, C_VEPS:C_VEPS + 8]).then_inc(h_ev, 1)
                wA2.wait(10)
                v.tensor_mul(STATS[:, C_VAR2:C_VAR2 + 8], STATS[:, C_VAR:C_VAR + 8],
                             STATS[:, C_RV:C_RV + 8])
                v.drain()
                v.tensor_scalar(STATS[:, C_V2E:C_V2E + 8], STATS[:, C_VAR2:C_VAR2 + 8],
                                EPS, None, ALU.add)
                v.drain()
                v.reciprocal(STATS[:, C_RV2:C_RV2 + 8],
                             STATS[:, C_V2E:C_V2E + 8]).then_inc(h_ev, 1)
                wA2.wait(11)
                v.tensor_mul(STATS[:, C_A2:C_A2 + 8], STATS[:, C_R1:C_R1 + 8],
                             STATS[:, C_R2:C_R2 + 8])
                v.scalar_tensor_tensor(STATS[:, C_B1:C_B1 + 8], STATS[:, C_MU:C_MU + 8],
                                       -1.0, STATS[:, C_R1:C_R1 + 8], ALU.mult, ALU.mult)
                v.drain()
                v.scalar_tensor_tensor(STATS[:, C_B2:C_B2 + 8], STATS[:, C_MU:C_MU + 8],
                                       -1.0, STATS[:, C_A2:C_A2 + 8], ALU.mult,
                                       ALU.mult).then_inc(h_ev, 1)
                for tt in range(NTT):
                    wP.wait(2 * tt + 1)
                    v.tensor_copy(g3(XNFa, KC)[:, :, tt * 128:(tt + 1) * 128],
                                  g3(PTR.ap(), KC)).then_inc(h_ev, 1)
                if PH >= 2:
                    def qk_dest(i, hp):
                        ocidx, tb = divmod(i, 2)
                        if ocidx < 8:
                            return SHa[:, OFF_QF + ocidx * 1024 + tb * 512:
                                       OFF_QF + ocidx * 1024 + tb * 512 + 512]
                        kc0 = (ocidx - 8) * T_SEQ + hp * 1024 + tb * 512
                        return KFa[:, kc0:kc0 + 512]
                    def qk_min(i):
                        wA2.wait(AQ + i + 1)
                        if i >= 2:
                            wA2.wait(AQ + 32 + i - 1)
                        eludst = SHa[:, OFF_ELU + (i % 2) * 1024:OFF_ELU + (i % 2) * 1024 + 512]
                        if i < 16:
                            v.tensor_scalar(eludst, qk_dest(i, 0), 0.0, None,
                                            ALU.min).then_inc(h_ev, 1)
                        else:
                            with v.If_eq(vpr, 0):
                                v.tensor_scalar(eludst, qk_dest(i, 0), 0.0, None,
                                                ALU.min).then_inc(h_ev, 1)
                            with v.Else():
                                v.tensor_scalar(eludst, qk_dest(i, 1), 0.0, None,
                                                ALU.min).then_inc(h_ev, 1)

                    def qk_stt(i):
                        wA2.wait(AQ + 32 + i + 1)
                        elusrc = SHa[:, OFF_ELU + (i % 2) * 1024 + 512:
                                     OFF_ELU + (i % 2) * 1024 + 1024]
                        if i < 16:
                            v.scalar_tensor_tensor(qk_dest(i, 0), qk_dest(i, 0), 0.0, elusrc,
                                                   ALU.max, ALU.add).then_inc(h_ev, 1)
                        else:
                            with v.If_eq(vpr, 0):
                                v.scalar_tensor_tensor(qk_dest(i, 0), qk_dest(i, 0), 0.0, elusrc,
                                                       ALU.max, ALU.add).then_inc(h_ev, 1)
                            with v.Else():
                                v.scalar_tensor_tensor(qk_dest(i, 1), qk_dest(i, 1), 0.0, elusrc,
                                                       ALU.max, ALU.add).then_inc(h_ev, 1)
                    for i in range(32):
                        qk_min(i)
                        if i >= 2:
                            qk_stt(i - 2)
                    qk_stt(30)
                    qk_stt(31)
                    for tt in range(NTT):
                        for vb in range(2):
                            wP.wait(PQ + 32 + 2 * tt + vb + 1)
                            srcp = PM.ap()[:, vb * 512:vb * 512 + 512].rearrange(
                                "p (h o c) -> p h o c", h=8, o=1)
                            v4 = VAGa.rearrange("p (hx i cc) -> p hx i cc", hx=16, i=16)
                            with v.If_eq(vpr, 0):
                                v.tensor_copy(v4[:, vb * 8:vb * 8 + 8, tt:tt + 1, 0:64],
                                              srcp).then_inc(h_ev, 1)
                            with v.Else():
                                v.tensor_copy(v4[:, vb * 8:vb * 8 + 8, 8 + tt:9 + tt, 0:64],
                                              srcp).then_inc(h_ev, 1)
                    v.drain()
                if PH >= 3:
                    wP.advance(PA)
                    wA2.advance(AA)
                    for jj in range(2):
                        for pp in range(2):
                            p0 = pp * 64
                            for f in range(4):
                                for c in range(2):
                                    m = 2 * f + c
                                    hg = 2 * m + pp
                                    if not (f == 0):
                                        wP.wait(PA + jj * 296 + pp * 144 + 36 * f + 18 * c - 19)
                                    v.tensor_copy(
                                        SHa[p0:p0 + 64, OFF_KST + c * 2048:OFF_KST + c * 2048 + 2048],
                                        KFa[p0:p0 + 64, m * T_SEQ:m * T_SEQ + T_SEQ])
                                    v.tensor_copy(
                                        SHa[p0:p0 + 64, OFF_QST + c * 512:OFF_QST + c * 512 + 512],
                                        SHa[p0:p0 + 64, OFF_QF + m * 1024 + jj * 512:
                                            OFF_QF + m * 1024 + jj * 512 + 512])
                                    v.tensor_copy(
                                        sha(OFF_VST + c * 1040, 1040),
                                        VAGa[:, hg * 1040:hg * 1040 + 1040]).then_inc(h_ev, 1)
                                for c in range(2):
                                    for si in range(0, 16, 2):
                                        wP.wait(PA + jj * 296 + pp * 144 + 36 * f + c * 18 + si + 1)
                                        v.tensor_copy(SHa[:, OFF_ST + si * 512:OFF_ST + si * 512 + 512],
                                                      PST[:, 0:512]).then_inc(h_ev, 1)
                                    wP.wait(PA + jj * 296 + pp * 144 + 36 * f + c * 18 + 17)
                                    with nc.allow_low_precision(reason="bf16 recip for bf16 matmul"):
                                        v.reciprocal(RCPR[0:1, :], PY[64:65, :]).then_inc(h_ev, 1)
                                    wA2.wait(AA + jj * 144 + pp * 72 + 9 * (2 * f + c) + 9)
                                    v.tensor_tensor(
                                        SHa[p0:p0 + 64, OFF_YN + (2 * f + c) * 512:
                                            OFF_YN + (2 * f + c) * 512 + 512],
                                        PY[0:64, :], RCPB[0:64, :], ALU.mult).then_inc(h_ev, 1)
                        wP.advance(PA + jj * 296 + 288)
                        for tt in range(4):
                            for do in range(2):
                                wP.wait(PA + jj * 296 + 288 + tt * 2 + do + 1)
                                zc = jj * 4096 + tt * 1024 + do * 512
                                v.tensor_tensor(Za[:, zc:zc + 512], pmt(do),
                                                Za[:, zc:zc + 512], ALU.add).then_inc(h_ev, 1)
                    wP.advance(PG)
                if PH >= 4:
                    for tt in range(NTT):
                        wP.wait(PG + tt + 1)
                        v.reduce_max(ESCA[:, E_NMX + tt:E_NMX + tt + 1],
                                     PM[:, (tt % 2) * 512:(tt % 2) * 512 + 8],
                                     axis=AX, negate=True).then_inc(h_ev, 1)
                    wA2.wait(AX_)
                    esc3 = g3(ESCA[:, E_ESC:E_ESC + 64])
                    v.reduce_sum(g3(ESCA[:, E_ESUM:E_ESUM + 8], 8), esc3, axis=AX)
                    v.reduce_max(g3(ESCA[:, E_M1:E_M1 + 8], 8), esc3, axis=AX)
                    v.drain()
                    v.reciprocal(ESCA[:, E_RS:E_RS + 8], ESCA[:, E_ESUM:E_ESUM + 8])
                    v.tensor_tensor(g3(ESCA[:, E_EQ:E_EQ + 64]), esc3,
                                    bc3(g3(ESCA[:, E_M1:E_M1 + 8], 8), 8), ALU.is_equal)
                    v.drain()
                    v.tensor_scalar(ESCA[:, E_X2:E_X2 + 64], ESCA[:, E_EQ:E_EQ + 64],
                                    -1.0, 1.0, ALU.mult, ALU.add)
                    v.drain()
                    v.tensor_tensor(g3(ESCA[:, E_EQ:E_EQ + 64]), esc3,
                                    g3(ESCA[:, E_X2:E_X2 + 64]), ALU.mult)
                    v.drain()
                    v.reduce_max(g3(ESCA[:, E_M2:E_M2 + 8], 8), g3(ESCA[:, E_EQ:E_EQ + 64]), axis=AX)
                    v.drain()
                    v.tensor_tensor(g3(ESCA[:, E_X2:E_X2 + 64]), esc3,
                                    bc3(g3(ESCA[:, E_M2:E_M2 + 8], 8), 8), ALU.is_ge)
                    v.drain()
                    v.tensor_tensor(g3(ESCA[:, E_EQ:E_EQ + 64]), esc3,
                                    g3(ESCA[:, E_X2:E_X2 + 64]), ALU.mult)
                    v.drain()
                    wtm_t = WTM.ap().rearrange("p (e t) -> p t e", e=8)
                    v.tensor_tensor(wtm_t, g3(ESCA[:, E_EQ:E_EQ + 64]),
                                    bc3(g3(ESCA[:, E_RS:E_RS + 8], 8), 8),
                                    ALU.mult).then_inc(h_ev, 1)
                    v.drain()
                if PH >= 5:
                    if dbg and PH >= 3:
                        v.wait_ge(h_dbg, H * D_DBG + 16 * DBG_ZATT)
                    wP.advance(PX)

                    def v_w(w):
                        for c in range(2):
                            v.drain()
                            v.tensor_copy(WTC[:], WTM[:, ds((2 * w + c) * 8, 8)])
                            v.drain()
                            for yc in range(6):
                                for tb in range(2):
                                    wP.wait(PX + c * 40 + 12 + 2 * yc + tb + 1)
                                    v.tensor_tensor(SHa[:, OFF_A + tb * 3072 + yc * 512:
                                                        OFF_A + tb * 3072 + yc * 512 + 512],
                                                    pmt(tb),
                                                    SHa[:, OFF_SL + tb * 3072 + yc * 512:
                                                        OFF_SL + tb * 3072 + yc * 512 + 512],
                                                    ALU.mult).then_inc(h_ev, 1)
                            for tb in range(2):
                                for tt4 in range(4):
                                    for do in range(2):
                                        wP.wait(PX + c * 40 + 24 + tb * 8 + tt4 * 2 + do + 1)
                                        ti = tb * 4 + tt4
                                        zs = Za[:, ti * 1024 + do * 512:ti * 1024 + do * 512 + 512]
                                        zs2 = Za[:, ti * 1024 + do * 512:ti * 1024 + do * 512 + 512]
                                        v.scalar_tensor_tensor(
                                            zs, pmt(do), WTC[:, ti:ti + 1],
                                            zs2, ALU.mult, ALU.add).then_inc(h_ev, 1)
                    fori_walk(v, 4, [(wP, 80)], v_w)
                if PH >= 6:
                    wP.advance(PW + 24)
                    for u in range(6):
                        for c in range(2):
                            oc = 2 * u + c
                            for tb in range(2):
                                wP.wait(PW + 24 + 2 * oc + tb + 1)
                                v.tensor_tensor(SHa[:, OFF_AS + tb * 6144 + oc * 512:
                                                    OFF_AS + tb * 6144 + oc * 512 + 512],
                                                pmt(tb),
                                                SHa[:, OFF_GS + tb * 6144 + oc * 512:
                                                    OFF_GS + tb * 6144 + oc * 512 + 512],
                                                ALU.mult).then_inc(h_ev, 1)
                    v.drain()
                    for tb in range(2):
                        for tt4 in range(4):
                            for do in range(2):
                                wP.wait(PW + 48 + tb * 8 + tt4 * 2 + do + 1)
                                ti = tb * 4 + tt4
                                zs = Za[:, ti * 1024 + do * 512:ti * 1024 + do * 512 + 512]
                                zs2 = Za[:, ti * 1024 + do * 512:ti * 1024 + do * 512 + 512]
                                v.tensor_tensor(zs, pmt(do), zs2, ALU.add).then_inc(h_ev, 1)

    return nc


def _prep_inputs(L0_attn_x, L0_attn_W1, L0_attn_W2, L0_ffn_gate_w, L0_ffn_W1,
                 L0_ffn_W2, L0_ffn_Ws1, L0_ffn_Ws2):
    xf = np.ascontiguousarray(np.asarray(L0_attn_x, dtype=np.float32))
    cm2 = ((np.arange(896)[None, :] - 384) >= np.arange(128)[:, None]).astype(BF)
    shared = dict(
        w1aT=np.ascontiguousarray(np.asarray(L0_attn_W1, np.float32).T),
        w2aT=np.ascontiguousarray(np.asarray(L0_attn_W2, np.float32).T),
        gwT=np.ascontiguousarray(np.asarray(L0_ffn_gate_w, np.float32).T),
        w1eT=np.ascontiguousarray(np.asarray(L0_ffn_W1, np.float32).transpose(0, 2, 1)),
        w2eT=np.ascontiguousarray(np.asarray(L0_ffn_W2, np.float32).transpose(0, 2, 1)),
        ws1T=np.ascontiguousarray(np.asarray(L0_ffn_Ws1, np.float32).T),
        ws2T=np.ascontiguousarray(np.asarray(L0_ffn_Ws2, np.float32).T),
        identc=np.eye(128, dtype=BF),
        cmask2=cm2,
    )
    return xf, shared


def kernel(L0_attn_x, L0_attn_W1, L0_attn_W2, L0_ffn_gate_w, L0_ffn_W1,
           L0_ffn_W2, L0_ffn_Ws1, L0_ffn_Ws2):
    xf, shared = _prep_inputs(L0_attn_x, L0_attn_W1, L0_attn_W2, L0_ffn_gate_w,
                              L0_ffn_W1, L0_ffn_W2, L0_ffn_Ws1, L0_ffn_Ws2)
    nc = build_program()
    in_maps = [dict(x=xf[2 * c:2 * c + 2], **shared) for c in range(8)]
    res = run_bass_kernel_spmd(nc, in_maps, core_ids=list(range(8)))
    return np.concatenate([r["out"] for r in res.results], axis=0)
